# version 23
# speedup vs baseline: 1.2680x; 1.0029x over previous
import math
import numpy as np
import ml_dtypes
from contextlib import ExitStack

import concourse.bass as bass
import concourse.mybir as mybir
from concourse.bass_utils import run_bass_kernel_spmd

F32 = mybir.dt.float32
BF16 = mybir.dt.bfloat16
AF = mybir.ActivationFunctionType
ALU = mybir.AluOpType
AXX = mybir.AxisListType.X

NC = 8
D = 4096
H = 32
T = 1024
CH = 512
FC = 32
NHG = 16
SCALE = 1.0 / math.sqrt(128.0)
EPS = 1e-5
NEG = -1e30
NJ = (7, 15)
NSL = (NJ[0] + 1, NJ[1] + 1)


def _owner(j):
    return (j, 0) if j <= 7 else (15 - j, 1)


def build_program(detect_races=True, stop_phase=99, ksub=0, dbg=False):
    KSUB = [ksub]
    DK = "ExternalOutput" if dbg else "Internal"
    nc = bass.Bass("TRN2", target_bir_lowering=False,
                   detect_race_conditions=detect_races)

    x_own = nc.dram_tensor("x_own", [T, D], F32, kind="ExternalInput").ap()
    xT_own = nc.dram_tensor("xT_own", [D, T], F32, kind="ExternalInput").ap()
    wq_t = nc.dram_tensor("wq_t", [128, FC, D], F32, kind="ExternalInput").ap()
    wk_t = nc.dram_tensor("wk_t", [128, FC, D], F32, kind="ExternalInput").ap()
    wv_t = nc.dram_tensor("wv_t", [128, FC, D], F32, kind="ExternalInput").ap()
    wo_t = nc.dram_tensor("wo_t", [128, FC, D], F32, kind="ExternalInput").ap()
    wb_t = nc.dram_tensor("wb_t", [128, FC, D], F32, kind="ExternalInput").ap()
    wr_t = nc.dram_tensor("wr_t", [128, FC * 8], F32, kind="ExternalInput").ap()
    a_t = nc.dram_tensor("a_t", [128, FC * 128], F32, kind="ExternalInput").ap()
    b_t = nc.dram_tensor("b_t", [128, D], F32, kind="ExternalInput").ap()
    bb_t = nc.dram_tensor("bb_t", [128, FC], F32, kind="ExternalInput").ap()
    cos_in = nc.dram_tensor("cos_in", [128, T], F32, kind="ExternalInput").ap()
    sin_in = nc.dram_tensor("sin_in", [128, T], F32, kind="ExternalInput").ap()
    kill_in = nc.dram_tensor("kill_in", [1, 32], F32, kind="ExternalInput").ap()
    outT = nc.dram_tensor("outT", [D, T], F32, kind="ExternalOutput").ap()

    kvK_loc = nc.dram_tensor("kvK_loc", [D, T], BF16, kind="Internal")
    kvV_loc = nc.dram_tensor("kvV_loc", [T, D], BF16, kind="Internal")
    k2_loc = nc.dram_tensor("k2_loc", [1, 32], F32, kind="Internal")
    kvK_all = nc.dram_tensor("kvK_all", [NC * D, T], BF16, kind="Internal")
    kvV_all = nc.dram_tensor("kvV_all", [NC * T, D], BF16, kind="Internal")
    k2_all = nc.dram_tensor("k2_all", [NC, 32], F32, kind="Internal")
    k2_fl = nc.dram_tensor("k2_fl", [NC, 32], F32, kind="Internal")
    g_sp = nc.dram_tensor("g_sp", [D, T], BF16, kind=DK)
    xnT_dbg = nc.dram_tensor("xnT_dbg", [128, FC * T], BF16, kind=DK) if dbg else None
    q_dbg = nc.dram_tensor("q_dbg", [128, FC * T], BF16, kind=DK) if dbg else None

    ident_np = np.eye(128, dtype=ml_dtypes.bfloat16)
    mexp_np = np.zeros((8, 128), dtype=ml_dtypes.bfloat16)
    for e in range(8):
        mexp_np[e, e * 16:(e + 1) * 16] = 1.0
    ones65f_np = np.ones((65, 128), dtype=np.float32)
    ones1b_np = np.ones((1, 128), dtype=ml_dtypes.bfloat16)
    ones128_np = np.ones((128, 1), dtype=ml_dtypes.bfloat16)
    tri_np = np.zeros((128, 4 * CH), dtype=ml_dtypes.bfloat16)
    for kt in range(4):
        for kk in range(128):
            q0 = kt * 128 + kk
            if q0 < CH:
                tri_np[kk, kt * CH + q0:(kt + 1) * CH] = 1.0
    ident_d = nc.inline_tensor(ident_np, "ident_c")
    mexp_d = nc.inline_tensor(mexp_np, "mexp_c")
    ones65f_d = nc.inline_tensor(ones65f_np, "ones65f_c")
    ones1b_d = nc.inline_tensor(ones1b_np, "ones1b_c")
    ones128_d = nc.inline_tensor(ones128_np, "ones128_c")
    tri_d = nc.inline_tensor(tri_np, "tri_c")

    ctx = ExitStack()
    with ctx:
        ident = ctx.enter_context(nc.sbuf_tensor("ident", [128, 128], BF16))
        mexp = ctx.enter_context(nc.sbuf_tensor("mexp", [8, 128], BF16))
        ones65f = ctx.enter_context(nc.sbuf_tensor("ones65f", [65, 128], F32))
        ones1b = ctx.enter_context(nc.sbuf_tensor("ones1b", [1, 128], BF16))
        ones128 = ctx.enter_context(nc.sbuf_tensor("ones128", [128, 1], BF16))
        bbs = ctx.enter_context(nc.sbuf_tensor("bbs", [128, FC], F32))
        kills = ctx.enter_context(nc.sbuf_tensor("kills", [1, 32], F32))
        csb = ctx.enter_context(nc.sbuf_tensor("csb", [128, T], BF16))
        snb = ctx.enter_context(nc.sbuf_tensor("snb", [128, T], BF16))
        mq2 = ctx.enter_context(nc.sbuf_tensor("mq2", [1, 64], F32))
        k2s = ctx.enter_context(nc.sbuf_tensor("k2s", [1, 256], F32))
        mk2own = ctx.enter_context(nc.sbuf_tensor("mk2own", [1, 32], F32))
        t2a = ctx.enter_context(nc.sbuf_tensor("t2a", [1, 8], F32))
        brow = ctx.enter_context(nc.sbuf_tensor("brow", [1, 544], BF16))
        b1s = ctx.enter_context(nc.sbuf_tensor("b1s", [1, 64], F32))
        bpair = ctx.enter_context(nc.sbuf_tensor("bpair", [1, 32], F32))
        biasb = ctx.enter_context(nc.sbuf_tensor("biasb", [128, 544], F32))
        stat = ctx.enter_context(nc.sbuf_tensor("stat", [128, 20], F32))
        rscr = ctx.enter_context(nc.sbuf_tensor("rscr", [128, 64], F32))
        ar2 = ctx.enter_context(nc.sbuf_tensor("ar2", [128, FC * T], BF16))
        xnT = ctx.enter_context(nc.sbuf_tensor("xnT", [128, FC * T], BF16))

        def _early():
            return nc

        class _S:
            def __init__(self, name):
                self.h = ctx.enter_context(nc.semaphore(name))
                self.n = 0

        sX = _S("sX")
        sW = [_S(f"sW{i}") for i in range(4)]
        sP = [_S(f"sP{i}") for i in range(8)]
        sCD = _S("sCD")
        sSt = [_S(f"sSt{i}") for i in range(3)]
        sSv = [_S(f"sSv{i}") for i in range(3)]
        sG = _S("sG")
        sSw = _S("sSw")
        sCK = _S("sCK")
        sCV = _S("sCV")
        sC2 = _S("sC2")
        sOut = _S("sOut")
        sT = _S("sT")
        sTc = _S("sTc")
        sTa = _S("sTa")
        sV = _S("sV")
        sSc = _S("sSc")
        sSe = _S("sSe")
        sVm = _S("sVm")

        p1 = ExitStack()
        with p1:
            xbuf = [p1.enter_context(nc.sbuf_tensor(f"xbuf{i}", [128, D], F32)) for i in range(2)]
            xntok = p1.enter_context(nc.sbuf_tensor("xntok", [128, D], BF16))
            sqf = p1.enter_context(nc.sbuf_tensor("sqf", [128, D], F32))
            ptr = [p1.enter_context(nc.psum_tensor(f"ptr{i}", [128, 1024], BF16)) for i in range(2)]

            VI1 = 14
            bX, bV, bSc, bT0 = sX.n, sV.n, sSc.n, sT.n
            pos_sq = {}
            pos_sqrt = {}
            _c = bSc
            for _t in range(8):
                _c += 1
                pos_sq[_t] = _c
                if _t >= 1:
                    _c += 1
                    pos_sqrt[_t - 1] = _c
            _c += 1
            pos_sqrt[7] = _c
            with nc.Block() as blk:
                consts = ((ident_d.ap(), ident[:, :]),
                          (mexp_d.ap(), mexp[:, :]), (ones65f_d.ap(), ones65f[:, :]),
                          (ones1b_d.ap(), ones1b[:, :]),
                          (ones128_d.ap(), ones128[:, :]),
                          (bb_t, bbs[:, :]), (kill_in, kills[:, :]))
                NCON = len(consts)

                @blk.sync
                def _(q):
                    for src, dst in consts:
                        q.dma_start(out=dst, in_=src).then_inc(sX.h, 16)
                    for tt in range(8):
                        if tt >= 2:
                            q.wait_ge(sV.h, bV + VI1 * (tt - 2) + 10)
                        q.dma_start(out=xbuf[tt % 2][0:64, :],
                                    in_=x_own[tt * 128:tt * 128 + 64, :]).then_inc(sX.h, 16)
                        q.dma_start(out=xbuf[tt % 2][64:128, :],
                                    in_=x_own[tt * 128 + 64:(tt + 1) * 128, :]
                                    ).then_inc(sX.h, 16)
                    q.wait_ge(sV.h, bV + VI1 * 6 + 10)
                    q.dma_start(out=xbuf[0][:, 0:T], in_=cos_in).then_inc(sX.h, 16)
                    q.dma_start(out=xbuf[0][:, T:2 * T], in_=sin_in).then_inc(sX.h, 16)
                sX.n += (NCON + 16 + 2) * 16
                XL0 = bX + NCON * 16

                @blk.scalar
                def _(sc):
                    def emit_sqrt(t):
                        p = (t % 2) * 10
                        sc.wait_ge(sV.h, bV + VI1 * t + 8)
                        sc.activation(out=stat[:, p + 8:p + 9], in_=stat[:, p + 7:p + 8],
                                      func=AF.Sqrt).then_inc(sSc.h, 1)
                    for tt in range(8):
                        sc.wait_ge(sX.h, XL0 + (tt + 1) * 32)
                        if tt >= 1:
                            sc.wait_ge(sV.h, bV + VI1 * (tt - 1) + 2)
                        sc.activation(out=sqf[:, :], in_=xbuf[tt % 2][:, :],
                                      func=AF.Square).then_inc(sSc.h, 1)
                        if tt >= 1:
                            emit_sqrt(tt - 1)
                    emit_sqrt(7)
                    sc.wait_ge(sX.h, XL0 + 8 * 32 + 2 * 16)
                    sc.copy(csb[:, :], xbuf[0][:, 0:T]).then_inc(sSc.h, 1)
                    sc.copy(snb[:, :], xbuf[0][:, T:2 * T]).then_inc(sSc.h, 1)
                sSc.n += 18

                @blk.vector
                def _(v):
                    for tt in range(8):
                        b = bV + VI1 * tt
                        p = (tt % 2) * 10
                        xb = xbuf[tt % 2]
                        v.wait_ge(sX.h, XL0 + (tt + 1) * 32)
                        v.reduce_sum(stat[:, p + 0:p + 1], xb[:, :], axis=AXX).then_inc(sV.h, 1)
                        v.wait_ge(sSc.h, pos_sq[tt])
                        v.reduce_sum(stat[:, p + 2:p + 3], sqf[:, :], axis=AXX).then_inc(sV.h, 1)
                        v.wait_ge(sV.h, b + 1)
                        v.tensor_scalar(out=stat[:, p + 1:p + 2], in0=stat[:, p + 0:p + 1],
                                        scalar1=1.0 / D, scalar2=None,
                                        op0=ALU.mult).then_inc(sV.h, 1)
                        v.wait_ge(sV.h, b + 2)
                        v.tensor_scalar(out=stat[:, p + 3:p + 4], in0=stat[:, p + 2:p + 3],
                                        scalar1=1.0 / D, scalar2=None,
                                        op0=ALU.mult).then_inc(sV.h, 1)
                        v.wait_ge(sV.h, b + 3)
                        v.tensor_mul(stat[:, p + 4:p + 5], stat[:, p + 1:p + 2],
                                     stat[:, p + 1:p + 2]).then_inc(sV.h, 1)
                        v.wait_ge(sV.h, b + 5)
                        v.tensor_sub(stat[:, p + 5:p + 6], stat[:, p + 3:p + 4],
                                     stat[:, p + 4:p + 5]).then_inc(sV.h, 1)
                        v.wait_ge(sV.h, b + 6)
                        v.tensor_scalar(out=stat[:, p + 6:p + 7], in0=stat[:, p + 5:p + 6],
                                        scalar1=EPS, scalar2=None,
                                        op0=ALU.add).then_inc(sV.h, 1)
                        v.wait_ge(sV.h, b + 7)
                        v.reciprocal(stat[:, p + 7:p + 8], stat[:, p + 6:p + 7]).then_inc(sV.h, 1)
                        v.wait_ge(sSc.h, pos_sqrt[tt])
                        v.tensor_scalar(out=xntok[:, :], in0=xb[:, :],
                                        scalar1=stat[:, p + 1:p + 2], scalar2=stat[:, p + 8:p + 9],
                                        op0=ALU.subtract, op1=ALU.mult).then_inc(sV.h, 2)
                        for g in range(4):
                            v.wait_ge(sT.h, bT0 + 4 * tt + g + 1)
                            dst = xnT[:, :].rearrange("p (dc t) -> p dc t", dc=FC)[
                                :, g * 8:(g + 1) * 8, tt * 128:(tt + 1) * 128]
                            v.tensor_copy(
                                dst, ptr[g % 2][:, :].rearrange("p (dc t) -> p dc t", dc=8)
                            ).then_inc(sV.h, 1)
                sV.n += 8 * VI1

                @blk.tensor
                def _(te):
                    for tt in range(8):
                        te.wait_ge(sV.h, bV + VI1 * tt + 10)
                        for g in range(4):
                            G = 4 * tt + g
                            if G >= 2:
                                Gp = G - 2
                                te.wait_ge(sV.h, bV + VI1 * (Gp // 4) + 10 + (Gp % 4) + 1)
                            last = None
                            for i in range(8):
                                dc = g * 8 + i
                                last = te.transpose(ptr[g % 2][:, i * 128:(i + 1) * 128],
                                                    xntok[:, dc * 128:(dc + 1) * 128],
                                                    ident[:, :])
                            last.then_inc(sT.h, 1)
                sT.n += 32
            P1_V, P1_SC = sV.n, sSc.n
            if stop_phase <= 1:
                return _early()

        p2 = ExitStack()
        with p2:
            w32 = [p2.enter_context(nc.sbuf_tensor(f"w32_{i}", [128, 1024], F32)) for i in range(4)]
            wb = [p2.enter_context(nc.sbuf_tensor(f"wb_{i}", [128, D], BF16)) for i in range(3)]
            kswap = [p2.enter_context(nc.sbuf_tensor(f"kswap{i}", [128, T], BF16)) for i in range(2)]
            sq2 = p2.enter_context(nc.sbuf_tensor("sq2", [128, T], BF16))
            atb = p2.enter_context(nc.sbuf_tensor("atb", [128, FC * 128], BF16))
            btb = p2.enter_context(nc.sbuf_tensor("btb", [128, D], BF16))
            wrb = p2.enter_context(nc.sbuf_tensor("wrb", [128, FC * 8], BF16))
            c2all = p2.enter_context(nc.sbuf_tensor("c2all", [128, 64], BF16))
            c2T = p2.enter_context(nc.sbuf_tensor("c2T", [8, T], BF16))
            h2 = p2.enter_context(nc.sbuf_tensor("h2", [128, T], BF16))
            gstage = kswap
            pp = [p2.enter_context(nc.psum_tensor(f"pp{i}", [128, CH], F32)) for i in range(4)]
            pk2 = [p2.enter_context(nc.psum_tensor(f"pk2_{i}", [1, CH], F32)) for i in range(2)]
            ptr2 = [p2.enter_context(nc.psum_tensor(f"ptr2_{i}", [128, 1024], BF16)) for i in range(2)]

            bSc = sSc.n
            bWsm = [sW[i].n for i in range(4)]
            sm_srcs = ([(a_t[:, i * 1024:(i + 1) * 1024], atb[:, i * 1024:(i + 1) * 1024])
                        for i in range(4)] +
                       [(b_t[:, i * 1024:(i + 1) * 1024], btb[:, i * 1024:(i + 1) * 1024])
                        for i in range(4)] +
                       [(wr_t[:, 0:FC * 8], wrb[:, :])])
            with nc.Block() as blk:
                @blk.sync
                def _(q):
                    q.wait_ge(sV.h, P1_V)
                    q.wait_ge(sSc.h, P1_SC)
                    for m, (src, dst) in enumerate(sm_srcs):
                        if m >= 4:
                            q.wait_ge(sSc.h, bSc + m - 3)
                        w = w32[m % 4][:, 0:src.shape[1]]
                        q.dma_start(out=w, in_=src).then_inc(sW[m % 4].h, 16)
                        sW[m % 4].n += 16

                @blk.scalar
                def _(sc):
                    for m, (src, dst) in enumerate(sm_srcs):
                        par = m % 4
                        sc.wait_ge(sW[par].h, bWsm[par] + 16 * (m // 4 + 1))
                        sc.copy(dst, w32[par][:, 0:src.shape[1]]).then_inc(sSc.h, 1)
                sSc.n += len(sm_srcs)
            if stop_phase <= 2:
                return _early()

            def qkv_phase(wsrc, kind):
                KS = KSUB[0] if kind == "k" else 0
                do_rope = kind in ("k", "q") and KS < 5
                do_k2 = kind in ("k", "q") and KS < 4
                do_pub = kind == "k" and KS < 3
                do_agk = kind == "k" and KS < 2
                do_ag2 = kind == "k" and KS < 1
                bV, bSc, bT0, bSw, bG = sV.n, sSc.n, sT.n, sSw.n, sG.n
                bW = [sW[i].n for i in range(4)]
                bP = [sP[i].n for i in range(8)]
                sc_per_f = 4
                v_per_f = (2 + (1 if do_k2 else 0) + (3 if do_rope else 0)
                           + ((3 if kind == "k" else 2) if do_k2 else 0))
                t_per_f = 1 + (1 if do_k2 else 0)
                sq_done = 3
                rope_done = 2 + (1 if do_k2 else 0) + (3 if do_rope else 0)

                def k2_done_sT(f):
                    return bT0 + (2 * f + 3 if f < FC - 1 else 2 * FC)

                with nc.Block() as blk:
                    @blk.sync
                    def _(q):
                        for f in range(FC):
                            for hh in range(4):
                                m = 4 * f + hh
                                if m >= 4:
                                    mp = m - 4
                                    q.wait_ge(sSc.h, bSc + sc_per_f * (mp // 4) + (mp % 4) + 1)
                                srcw = wsrc[:, f, hh * 1024:(hh + 1) * 1024]
                                q.dma_start(out=w32[m % 4][0:64, :],
                                            in_=srcw[0:64, :]).then_inc(sW[m % 4].h, 16)
                                q.dma_start(out=w32[m % 4][64:128, :],
                                            in_=srcw[64:128, :]).then_inc(sW[m % 4].h, 16)
                                sW[m % 4].n += 32

                    @blk.scalar
                    def _(sc):
                        for f in range(FC):
                            for hh in range(4):
                                m = 4 * f + hh
                                par = m % 4
                                sc.wait_ge(sW[par].h, bW[par] + 32 * (m // 4 + 1))
                                if f >= 3 and hh == 0:
                                    sc.wait_ge(sT.h, bT0 + t_per_f * (f - 2))
                                sc.copy(wb[f % 3][:, hh * 1024:(hh + 1) * 1024],
                                        w32[par][:, :]).then_inc(sSc.h, 1)
                        sSc.n += FC * sc_per_f

                    def emit_swaps(g, f):
                        g.wait_ge(sV.h, bV + v_per_f * f + 2)
                        if f >= 2:
                            g.wait_ge(sV.h, bV + v_per_f * (f - 2) + rope_done)
                        g.dma_start(out=kswap[f % 2][0:64, :],
                                    in_=ar2[64:128, f * T:(f + 1) * T]).then_inc(sSw.h, 16)
                        g.dma_start(out=kswap[f % 2][64:128, :],
                                    in_=ar2[0:64, f * T:(f + 1) * T]).then_inc(sSw.h, 16)
                    if do_rope:
                        sSw.n += FC * 32

                    @blk.tensor
                    def _(te):
                        te.wait_ge(sV.h, P1_V)
                        if kind == "q":
                            te.wait_ge(sT.h, VQ_BARRIER[0])
                        def k2mm(f):
                            te.wait_ge(sV.h, bV + v_per_f * f + sq_done)
                            if f >= 1:
                                te.wait_ge(sV.h, bV + v_per_f * f)
                            te.matmul(pk2[0][:, :], ones128[:, :], sq2[:, 0:CH],
                                      start=True, stop=True)
                            te.matmul(pk2[1][:, :], ones128[:, :], sq2[:, CH:T],
                                      start=True, stop=True).then_inc(sT.h, 1)
                        for f in range(FC):
                            te.wait_ge(sSc.h, bSc + sc_per_f * f + 4)
                            if f >= 2:
                                te.wait_ge(sV.h, bV + v_per_f * (f - 2) + 2)
                            for dc in range(FC):
                                st = dc == 0
                                sp = dc == FC - 1
                                te.matmul(pp[2 * (f % 2)][:, :],
                                          wb[f % 3][:, dc * 128:(dc + 1) * 128],
                                          xnT[:, dc * T:dc * T + CH], start=st, stop=sp)
                                last = te.matmul(pp[2 * (f % 2) + 1][:, :],
                                                 wb[f % 3][:, dc * 128:(dc + 1) * 128],
                                                 xnT[:, dc * T + CH:dc * T + T], start=st, stop=sp)
                            last.then_inc(sT.h, 1)
                            if do_k2 and f >= 1:
                                k2mm(f - 1)
                        if do_k2:
                            k2mm(FC - 1)
                        sT.n += FC * t_per_f

                    @blk.vector
                    def _(v):
                        for f in range(FC):
                            b0 = bV + v_per_f * f
                            v.wait_ge(sT.h, bT0 + t_per_f * f + 1)
                            if kind == "v":
                                v.wait_ge(sP[f // 4].h, (f % 4 + 1) * 32)
                            v.tensor_copy(ar2[:, f * T:f * T + CH], pp[2 * (f % 2)][:, :])
                            v.tensor_copy(ar2[:, f * T + CH:(f + 1) * T],
                                          pp[2 * (f % 2) + 1][:, :]).then_inc(sV.h, 2)
                            if do_k2:
                                v.wait_ge(sV.h, b0 + 2)
                                v.tensor_mul(sq2[:, :], ar2[:, f * T:(f + 1) * T],
                                             ar2[:, f * T:(f + 1) * T]).then_inc(sV.h, 1)
                            if do_rope:
                                if f == 0:
                                    v.wait_ge(sSc.h, P1_SC)
                                v.wait_ge(sSw.h, bSw + 32 * (f + 1))
                                ks = kswap[f % 2]
                                fs = ar2[:, f * T:(f + 1) * T]
                                v.tensor_mul(ks[:, :], ks[:, :], snb[:, :]).then_inc(sV.h, 1)
                                v.tensor_mul(fs, fs, csb[:, :]).then_inc(sV.h, 1)
                                v.wait_ge(sV.h, b0 + sq_done + 2)
                                v.tensor_add(fs, fs, ks[:, :]).then_inc(sV.h, 1)
                            if do_k2:
                                v.wait_ge(sT.h, k2_done_sT(f))
                                if kind == "k":
                                    v.reduce_max(t2a[:, 0:1], pk2[0][:, :],
                                                 axis=AXX).then_inc(sV.h, 1)
                                    v.reduce_max(t2a[:, 1:2], pk2[1][:, :],
                                                 axis=AXX).then_inc(sV.h, 1)
                                    v.wait_ge(sV.h, b0 + 8)
                                    v.tensor_max(mk2own[:, f:f + 1], t2a[:, 0:1],
                                                 t2a[:, 1:2]).then_inc(sV.h, 1)
                                else:
                                    v.reduce_max(mq2[:, f:f + 1], pk2[0][:, :],
                                                 axis=AXX).then_inc(sV.h, 1)
                                    v.reduce_max(mq2[:, 32 + f:33 + f], pk2[1][:, :],
                                                 axis=AXX).then_inc(sV.h, 1)
                        sV.n += FC * v_per_f

                    if do_pub:
                        @blk.gpsimd
                        def _(g):
                            for f in range(FC):
                                emit_swaps(g, f)
                                if f >= 1:
                                    fp = f - 1
                                    g.wait_ge(sV.h, bV + v_per_f * fp + rope_done)
                                    kv5 = kvK_loc.ap().rearrange(
                                        "(hg pos p) (hi c) -> hg pos p hi c",
                                        hg=16, pos=2, hi=2)
                                    for pos in range(2):
                                        g.dma_start(
                                            out=kv5[fp // 2, pos, :, fp % 2, :],
                                            in_=ar2[:, fp * T + pos * CH:
                                                    fp * T + (pos + 1) * CH],
                                        ).then_inc(sP[fp // 4].h, 16)
                                        sP[fp // 4].n += 16
                                    if do_agk and fp % 4 == 3:
                                        ci = fp // 4
                                        g.wait_ge(sP[ci].h, 4 * 32)
                                        g.collective_compute(
                                            "AllGather", ALU.bypass,
                                            replica_groups=[list(range(NC))],
                                            ins=[kvK_loc.ap()[ci * 512:(ci + 1) * 512, :].opt()],
                                            outs=[kvK_all.ap().rearrange(
                                                "(c rd) t -> c rd t", c=8)[ci].opt()],
                                        ).then_inc(sCK.h, 1)
                                        sCK.n += 1
                            fp = FC - 1
                            g.wait_ge(sV.h, bV + v_per_f * fp + rope_done)
                            kv5 = kvK_loc.ap().rearrange(
                                "(hg pos p) (hi c) -> hg pos p hi c",
                                hg=16, pos=2, hi=2)
                            for pos in range(2):
                                g.dma_start(
                                    out=kv5[fp // 2, pos, :, fp % 2, :],
                                    in_=ar2[:, fp * T + pos * CH:
                                            fp * T + (pos + 1) * CH],
                                ).then_inc(sP[fp // 4].h, 16)
                                sP[fp // 4].n += 16
                            if do_agk:
                                ci = fp // 4
                                g.wait_ge(sP[ci].h, 4 * 32)
                                g.collective_compute(
                                    "AllGather", ALU.bypass,
                                    replica_groups=[list(range(NC))],
                                    ins=[kvK_loc.ap()[ci * 512:(ci + 1) * 512, :].opt()],
                                    outs=[kvK_all.ap().rearrange(
                                        "(c rd) t -> c rd t", c=8)[ci].opt()],
                                ).then_inc(sCK.h, 1)
                                sCK.n += 1
                            if do_ag2:
                                g.wait_ge(sV.h, bV + FC * v_per_f)
                                g.dma_start(out=k2_loc.ap()[:, :],
                                            in_=mk2own[:, :]).then_inc(sG.h, 16)
                                sG.n += 16
                                g.wait_ge(sG.h, sG.n)
                                g.collective_compute(
                                    "AllGather", ALU.bypass,
                                    replica_groups=[list(range(NC))],
                                    ins=[k2_loc.ap().opt()], outs=[k2_all.ap().opt()],
                                ).then_inc(sC2.h, 1)
                                sC2.n += 1

                    if kind == "q":
                        @blk.gpsimd
                        def _(g):
                            for f in range(FC):
                                emit_swaps(g, f)
                                if f % 4 == 0 and f // 4 < 8:
                                    ci = f // 4
                                    g.wait_ge(sP[ci].h, 12 * 16)
                                    g.collective_compute(
                                        "AllGather", ALU.bypass,
                                        replica_groups=[list(range(NC))],
                                        ins=[kvV_loc.ap()[ci * 128:(ci + 1) * 128, :].opt()],
                                        outs=[kvV_all.ap().rearrange(
                                            "(c rt) d -> c rt d", c=8)[ci].opt()],
                                    ).then_inc(sCV.h, 1)
                                    sCV.n += 1
                            g.collective_compute(
                                "AllGather", ALU.bypass,
                                replica_groups=[list(range(NC))],
                                ins=[k2_loc.ap().opt()], outs=[k2_fl.ap().opt()],
                            ).then_inc(sCD.h, 1)
                            sCD.n += 1

                    if kind == "v":
                        pass

            VQ_BARRIER = [None]

            qkv_phase(wk_t, "k")
            if stop_phase <= 3:
                return _early()

            qkv_phase(wv_t, "v")
            if stop_phase <= 4:
                return _early()
            V_EVAC_V = sV.n

            bV, bT0, bG = sV.n, sT.n, sG.n
            vq = [kswap[0], kswap[1], sq2, h2]
            with nc.Block() as blk:
                @blk.tensor
                def _(te):
                    te.wait_ge(sV.h, V_EVAC_V)
                    for tt in range(8):
                        for gg in range(4):
                            Gx = 4 * tt + gg
                            if Gx >= 2:
                                te.wait_ge(sV.h, bV + (Gx - 2) + 1)
                            last = None
                            for i in range(8):
                                f = gg * 8 + i
                                last = te.transpose(
                                    ptr2[Gx % 2][:, i * 128:(i + 1) * 128],
                                    ar2[:, f * T + tt * 128:f * T + (tt + 1) * 128],
                                    ident[:, :])
                            last.then_inc(sT.h, 1)
                sT.n += 32

                @blk.vector
                def _(v):
                    for tt in range(8):
                        for gg in range(4):
                            Gx = 4 * tt + gg
                            v.wait_ge(sT.h, bT0 + Gx + 1)
                            if tt > 0:
                                v.wait_ge(sP[tt - 1].h, 12 * 16)
                            v.tensor_copy(vq[gg][:, 0:1024], ptr2[Gx % 2][:, :]).then_inc(sV.h, 1)
                sV.n += 32

                @blk.scalar
                def _(sc):
                    for tt in range(8):
                        for gg in range(4):
                            sc.wait_ge(sV.h, bV + 4 * tt + gg + 1)
                            sc.dma_start(out=kvV_loc.ap()[tt * 128:(tt + 1) * 128,
                                                          gg * 1024:(gg + 1) * 1024],
                                         in_=vq[gg][:, 0:1024]).then_inc(sP[tt].h, 16)
                            sP[tt].n += 16
            VQ_BARRIER[0] = sT.n
            if stop_phase <= 5:
                return _early()

            qkv_phase(wq_t, "q")
            if stop_phase <= 6:
                return _early()
            Q_DONE_V = sV.n
            if dbg:
                bdbg = sG.n
                with nc.Block() as blk:
                    @blk.gpsimd
                    def _(g):
                        g.wait_ge(sV.h, P1_V)
                        g.dma_start(out=xnT_dbg.ap(), in_=xnT[:, :]).then_inc(sG.h, 16)
                        g.wait_ge(sV.h, Q_DONE_V)
                        g.dma_start(out=q_dbg.ap(), in_=ar2[:, :]).then_inc(sG.h, 16)
                sG.n += 32

            bV, bT0, bX = sV.n, sT.n, sX.n
            pbias = pp[0]
            pbias2 = pp[1]
            with nc.Block() as blk:
                @blk.sync
                def _(q):
                    q.wait_ge(sC2.h, 1)
                    q.dma_start(out=k2s[:, :],
                                in_=k2_all.ap().rearrange("r h -> (r h)").unsqueeze(0)
                                ).then_inc(sX.h, 16)
                sX.n += 16

                @blk.vector
                def _(v):
                    v.wait_ge(sX.h, bX + 16)
                    v.wait_ge(sV.h, Q_DONE_V)
                    v.tensor_max(k2s[:, 0:128], k2s[:, 0:128],
                                 k2s[:, 128:256]).then_inc(sV.h, 1)
                    v.wait_ge(sV.h, bV + 1)
                    v.tensor_max(k2s[:, 0:64], k2s[:, 0:64],
                                 k2s[:, 64:128]).then_inc(sV.h, 1)
                    v.wait_ge(sV.h, bV + 2)
                    v.tensor_max(k2s[:, 0:32], k2s[:, 0:32],
                                 k2s[:, 32:64]).then_inc(sV.h, 1)
                    v.wait_ge(sV.h, bV + 3)
                    v.tensor_add(b1s[:, :].rearrange("o (c h) -> o c h", c=2),
                                 mq2[:, :].rearrange("o (c h) -> o c h", c=2),
                                 k2s[:, 0:32].unsqueeze(1).broadcast_to([1, 2, 32])
                                 ).then_inc(sV.h, 1)
                    v.wait_ge(sV.h, bV + 4)
                    v.tensor_scalar(out=b1s[:, :], in0=b1s[:, :], scalar1=-SCALE / 2.0,
                                    scalar2=None, op0=ALU.mult).then_inc(sV.h, 1)
                    v.wait_ge(sV.h, bV + 5)
                    bv = b1s[:, :].rearrange("o (c g p) -> o c g p", c=2, g=16)
                    v.tensor_max(bpair[:, :].rearrange("o (c g) -> o c g", c=2),
                                 bv[:, :, :, 0:1].squeeze(3),
                                 bv[:, :, :, 1:2].squeeze(3)).then_inc(sV.h, 1)
                    v.wait_ge(sV.h, bV + 6)
                    v.tensor_add(
                        brow[:, 0:512].rearrange("o (c g j) -> o c g j", c=2, g=16),
                        bpair[:, :].rearrange("o (c g) -> o c g", c=2).unsqueeze(3)
                        .broadcast_to([1, 2, 16, 16]),
                        kills[:, :].rearrange("o (c j) -> o c j", c=2).unsqueeze(2)
                        .broadcast_to([1, 2, 16, 16])).then_inc(sV.h, 1)
                    v.tensor_copy(brow[:, 512:544], bpair[:, :]).then_inc(sV.h, 1)
                    v.wait_ge(sT.h, bT0 + 1)
                    v.tensor_copy(biasb[:, 0:512], pbias[:, :])
                    v.tensor_copy(biasb[:, 512:544], pbias2[:, 0:32]).then_inc(sV.h, 1)
                sV.n += 9

                @blk.tensor
                def _(te):
                    te.wait_ge(sV.h, bV + 8)
                    te.matmul(pbias[:, :], ones1b[:, :], brow[:, 0:512],
                              start=True, stop=True)
                    te.matmul(pbias2[:, 0:32], ones1b[:, :], brow[:, 512:544],
                              start=True, stop=True).then_inc(sT.h, 1)
                sT.n += 1
            BIAS_V = sV.n
            if stop_phase <= 7:
                return _early()

            plg = [pp[2], pp[3]]
            RVI = 12
            bV, bSc, bT0 = sV.n, sSc.n, sT.n
            with nc.Block() as blk:
                @blk.tensor
                def _(te):
                    te.wait_ge(sV.h, P1_V)
                    for tt in range(8):
                        if tt >= 2:
                            te.wait_ge(sV.h, bV + RVI * (tt - 2) + 1)
                        for dc in range(FC):
                            last = te.matmul(plg[tt % 2][:, 0:8],
                                             xnT[:, dc * T + tt * 128:dc * T + (tt + 1) * 128],
                                             wrb[:, dc * 8:(dc + 1) * 8],
                                             start=(dc == 0), stop=(dc == FC - 1))
                        last.then_inc(sT.h, 1)
                    for tt in range(8):
                        te.wait_ge(sV.h, bV + RVI * tt + 12)
                        last = te.transpose(ptr2[0][0:8, tt * 128:(tt + 1) * 128],
                                            c2all[:, tt * 8:(tt + 1) * 8], ident[:, :])
                    last.then_inc(sT.h, 1)
                sT.n += 9

                @blk.vector
                def _(v):
                    for tt in range(8):
                        b = bV + RVI * tt
                        p = (tt % 2) * 32
                        lgs = rscr[:, p + 0:p + 8]
                        d0 = rscr[:, p + 8:p + 16]
                        wk_ = rscr[:, p + 16:p + 24]
                        m1 = stat[:, 0:1]
                        m2 = stat[:, 1:2]
                        den = stat[:, 2:3]
                        rden = stat[:, 3:4]
                        v.wait_ge(sT.h, bT0 + tt + 1)
                        v.tensor_copy(lgs, plg[tt % 2][:, 0:8]).then_inc(sV.h, 1)
                        v.wait_ge(sV.h, b + 1)
                        v.reduce_max(m1, lgs, axis=AXX).then_inc(sV.h, 1)
                        v.wait_ge(sV.h, b + 2)
                        v.tensor_scalar(out=d0, in0=lgs, scalar1=m1, scalar2=None,
                                        op0=ALU.subtract).then_inc(sV.h, 1)
                        v.wait_ge(sV.h, b + 3)
                        v.tensor_scalar(out=wk_, in0=d0, scalar1=0.0, scalar2=-1e30,
                                        op0=ALU.is_ge, op1=ALU.mult).then_inc(sV.h, 1)
                        v.wait_ge(sV.h, b + 4)
                        v.tensor_add(wk_, wk_, d0).then_inc(sV.h, 1)
                        v.wait_ge(sV.h, b + 5)
                        v.reduce_max(m2, wk_, axis=AXX).then_inc(sV.h, 1)
                        v.wait_ge(sV.h, b + 6)
                        v.tensor_scalar(out=wk_, in0=d0, scalar1=m2, scalar2=None,
                                        op0=ALU.subtract).then_inc(sV.h, 1)
                        v.wait_ge(sV.h, b + 7)
                        v.tensor_scalar(out=wk_, in0=wk_, scalar1=0.0, scalar2=None,
                                        op0=ALU.is_ge).then_inc(sV.h, 1)
                        v.wait_ge(sV.h, b + 8)
                        v.wait_ge(sSc.h, bSc + tt + 1)
                        v.tensor_mul(wk_, wk_, stat[:, 4:12]).then_inc(sV.h, 1)
                        v.wait_ge(sV.h, b + 9)
                        v.reduce_sum(den, wk_, axis=AXX).then_inc(sV.h, 1)
                        v.wait_ge(sV.h, b + 10)
                        v.reciprocal(rden, den).then_inc(sV.h, 1)
                        v.wait_ge(sV.h, b + 11)
                        v.tensor_scalar(out=c2all[:, tt * 8:(tt + 1) * 8], in0=wk_,
                                        scalar1=rden, scalar2=2.0, op0=ALU.mult,
                                        op1=ALU.mult).then_inc(sV.h, 1)
                    v.wait_ge(sT.h, bT0 + 9)
                    v.tensor_copy(c2T[:, :], ptr2[0][0:8, 0:1024]).then_inc(sV.h, 1)
                sV.n += 8 * RVI + 1

                @blk.scalar
                def _(sc):
                    for tt in range(8):
                        p = (tt % 2) * 32
                        sc.wait_ge(sV.h, bV + RVI * tt + 3)
                        sc.activation(out=stat[:, 4:12], in_=rscr[:, p + 8:p + 16],
                                      func=AF.Exp).then_inc(sSc.h, 1)
                sSc.n += 8
            C2T_V = sV.n
            if stop_phase <= 8:
                return _early()

            bV, bSc, bT0 = sV.n, sSc.n, sT.n
            with nc.Block() as blk:
                @blk.tensor
                def _(te):
                    te.wait_ge(sV.h, C2T_V)
                    for hh in range(2):
                        for dc in range(FC):
                            te.matmul(pp[hh][:, :], atb[:, dc * 128:(dc + 1) * 128],
                                      xnT[:, dc * T + hh * CH:dc * T + (hh + 1) * CH],
                                      start=(dc == 0), stop=(dc == FC - 1))
                    for hh in range(2):
                        last = te.matmul(pp[2 + hh][:, :], mexp[:, :],
                                         c2T[:, hh * CH:(hh + 1) * CH],
                                         start=True, stop=True)
                    last.then_inc(sT.h, 1)
                sT.n += 1

                @blk.scalar
                def _(sc):
                    sc.wait_ge(sT.h, bT0 + 1)
                    sc.copy(sq2[:, 0:CH], pp[0][:, :])
                    sc.copy(sq2[:, CH:T], pp[1][:, :]).then_inc(sSc.h, 1)
                sSc.n += 1

                @blk.vector
                def _(v):
                    v.wait_ge(sT.h, bT0 + 1)
                    v.tensor_copy(kswap[0][:, 0:CH], pp[2][:, :])
                    v.tensor_copy(kswap[0][:, CH:T], pp[3][:, :]).then_inc(sV.h, 1)
                    v.wait_ge(sV.h, bV + 1)
                    v.wait_ge(sSc.h, bSc + 1)
                    v.tensor_mul(h2[:, :], sq2[:, 0:T], kswap[0][:, :]).then_inc(sV.h, 1)
                sV.n += 2
            H2_V = sV.n
            if stop_phase <= 9:
                return _early()

            bV, bSc, bT0, bG = sV.n, sSc.n, sT.n, sG.n
            bW = [sW[i].n for i in range(4)]
            with nc.Block() as blk:
                @blk.sync
                def _(q):
                    for f in range(FC):
                        for hh in range(4):
                            m = 4 * f + hh
                            if m >= 4:
                                mp = m - 4
                                q.wait_ge(sSc.h, bSc + 5 * (mp // 4) + (mp % 4) + 1)
                            srcw = wb_t[:, f, hh * 1024:(hh + 1) * 1024]
                            q.dma_start(out=w32[m % 4][0:64, :],
                                        in_=srcw[0:64, :]).then_inc(sW[m % 4].h, 16)
                            q.dma_start(out=w32[m % 4][64:128, :],
                                        in_=srcw[64:128, :]).then_inc(sW[m % 4].h, 16)
                            sW[m % 4].n += 32

                pos_cast = {}
                pos_silu = {}
                _c = 0
                for _f in range(FC):
                    _c += 4
                    pos_cast[_f] = _c
                    if _f >= 1:
                        _c += 1
                        pos_silu[_f - 1] = _c
                _c += 1
                pos_silu[FC - 1] = _c

                @blk.scalar
                def _(sc):
                    def emit_silu(fp):
                        sc.wait_ge(sT.h, bT0 + fp + 1)
                        if fp >= 2:
                            sc.wait_ge(sG.h, bG + 16 * (fp - 1))
                        sc.activation(out=gstage[fp % 2][:, 0:CH],
                                      in_=pp[2 * (fp % 2)][:, :],
                                      func=AF.Silu, bias=bbs[:, fp:fp + 1], scale=1.0)
                        sc.activation(out=gstage[fp % 2][:, CH:T],
                                      in_=pp[2 * (fp % 2) + 1][:, :],
                                      func=AF.Silu, bias=bbs[:, fp:fp + 1],
                                      scale=1.0).then_inc(sSc.h, 1)
                    for f in range(FC):
                        for hh in range(4):
                            m = 4 * f + hh
                            par = m % 4
                            sc.wait_ge(sW[par].h, bW[par] + 32 * (m // 4 + 1))
                            if f >= 3 and hh == 0:
                                sc.wait_ge(sT.h, bT0 + (f - 2))
                            sc.copy(wb[f % 3][:, hh * 1024:(hh + 1) * 1024],
                                    w32[par][:, :]).then_inc(sSc.h, 1)
                        if f >= 1:
                            emit_silu(f - 1)
                    emit_silu(FC - 1)
                sSc.n += FC * 5

                @blk.tensor
                def _(te):
                    te.wait_ge(sV.h, H2_V)
                    for f in range(FC):
                        te.wait_ge(sSc.h, bSc + pos_cast[f])
                        if f >= 2:
                            te.wait_ge(sSc.h, bSc + pos_silu[f - 2])
                        for dc in range(FC):
                            te.matmul(pp[2 * (f % 2)][:, :], wb[f % 3][:, dc * 128:(dc + 1) * 128],
                                      xnT[:, dc * T:dc * T + CH], start=(dc == 0), stop=False)
                            te.matmul(pp[2 * (f % 2) + 1][:, :],
                                      wb[f % 3][:, dc * 128:(dc + 1) * 128],
                                      xnT[:, dc * T + CH:dc * T + T], start=(dc == 0), stop=False)
                        te.matmul(pp[2 * (f % 2)][:, :], btb[:, f * 128:(f + 1) * 128],
                                  h2[:, 0:CH], start=False, stop=True)
                        te.matmul(pp[2 * (f % 2) + 1][:, :], btb[:, f * 128:(f + 1) * 128],
                                  h2[:, CH:T], start=False, stop=True).then_inc(sT.h, 1)
                sT.n += FC

                @blk.gpsimd
                def _(g):
                    for f in range(FC):
                        g.wait_ge(sSc.h, bSc + pos_silu[f])
                        g.dma_start(out=g_sp.ap()[f * 128:(f + 1) * 128, :],
                                    in_=gstage[f % 2][:, :]).then_inc(sG.h, 16)
                sG.n += FC * 16
            FFN_SC = sSc.n
            FFN_G = sG.n
            if stop_phase <= 10:
                return _early()


        p5 = ExitStack()
        with p5:
            tri_s = p5.enter_context(nc.sbuf_tensor("tri_s", [128, 4 * CH], BF16))
            rb2 = p5.enter_context(nc.sbuf_tensor("rb2", [128, CH], F32))
            rbbs2 = p5.enter_context(nc.sbuf_tensor("rbbs2", [128, 2 * CH], F32))
            attno = xnT[:, 0:H * CH]
            kst = [p5.enter_context(nc.sbuf_tensor(f"kst{i}", [128, 1024], BF16)) for i in range(3)]
            vst = [p5.enter_context(nc.sbuf_tensor(f"vst{i}", [128, 1024], BF16)) for i in range(3)]
            ptb = [p5.enter_context(nc.sbuf_tensor(f"ptb{i}", [128, 2048], BF16)) for i in range(2)]
            w32a = [p5.enter_context(nc.sbuf_tensor(f"w32a{i}", [128, 1024], F32)) for i in range(4)]
            wba = [xnT[:, H * CH + i * D:H * CH + (i + 1) * D] for i in range(3)]
            xtb = [p5.enter_context(nc.sbuf_tensor(f"xtb{i}", [128, CH], F32)) for i in range(2)]
            gtl = [p5.enter_context(nc.sbuf_tensor(f"gtl{i}", [128, CH], BF16)) for i in range(2)]
            sgf = [p5.enter_context(nc.sbuf_tensor(f"sgf{i}", [128, CH], F32)) for i in range(2)]
            otl = [p5.enter_context(nc.sbuf_tensor(f"otl{i}", [128, CH], F32)) for i in range(2)]
            psS = p5.enter_context(nc.psum_tensor("psS", [128, 2048], F32))
            psAVV = p5.enter_context(nc.psum_tensor("psAVV", [128, 2 * CH], F32))
            psDR2 = p5.enter_context(nc.psum_tensor("psDR2", [128, 2 * CH], F32))
            psDh = [psDR2[0:1, 0:CH], psDR2[64:65, 0:CH]]

            bX = sX.n
            with nc.Block() as blk:
                @blk.sync
                def _(q):
                    q.wait_ge(sG.h, FFN_G)
                    q.dma_start(out=tri_s[:, :], in_=tri_d.ap()).then_inc(sX.h, 16)
                sX.n += 16
            TRI_X = sX.n

            gki = [0]
            gmi = [0]

            for chv in range(2):
                nsl = NSL[chv]
                ngr = 2 * nsl
                bSe, bTs, bTa, bTc, bVm, bV = (sSe.n, sT.n, sTa.n, sTc.n,
                                               sVm.n, sV.n)
                gk0 = gki[0]
                gm0 = gmi[0]

                def slot_info(hg, m, chv=chv, nsl=nsl):
                    if m < NJ[chv]:
                        ow, pos = _owner(m)
                        return ("off", m, ow, pos)
                    return ("diag", None, None, chv)

                pos_sc = {}
                pos_bc = {}
                _cnt = bTs
                for _hg in range(NHG):
                    for _r in range(ngr):
                        _cnt += 1
                        pos_sc[(_hg, _r)] = _cnt
                        if _r == 0 and _hg >= 1:
                            _cnt += 1
                            pos_bc[(_hg - 1, 0)] = _cnt
                            _cnt += 1
                            pos_bc[(_hg - 1, 1)] = _cnt
                _cnt += 1
                pos_bc[(NHG - 1, 0)] = _cnt
                _cnt += 1
                pos_bc[(NHG - 1, 1)] = _cnt

                with nc.Block() as blk:
                    @blk.sync
                    def _(q, chv=chv, nsl=nsl, gm0=gm0):
                        if chv == 0:
                            q.wait_ge(sCK.h, 8)
                            q.wait_ge(sCV.h, 8)
                            q.wait_ge(sCD.h, 1)
                        for hg in range(NHG):
                            for m in range(nsl):
                                M = gm0 + hg * nsl + m
                                if M >= 3:
                                    q.wait_ge(sTc.h, M - 3 + 1)
                                kind, j, ow, pos = slot_info(hg, m)
                                if kind == "off":
                                    ksrc = kvK_all.ap().rearrange(
                                        "(c r d) t -> c r d t", c=8, r=NC)[
                                        hg // 2, ow,
                                        (hg % 2) * 256 + pos * 128:
                                        (hg % 2) * 256 + (pos + 1) * 128, :]
                                else:
                                    ksrc = kvK_loc.ap()[
                                        hg * 256 + chv * 128:
                                        hg * 256 + (chv + 1) * 128, :]
                                q.dma_start(
                                    out=kst[M % 3][:, :],
                                    in_=ksrc,
                                ).then_inc(sSt[M % 3].h, 16)
                                if kind == "off":
                                    vsrc = kvV_all.ap().rearrange(
                                        "(c r t) d -> c r t d", c=8, r=NC)[
                                        4 * pos:4 * pos + 4, ow, :,
                                        hg * 256:(hg + 1) * 256]
                                    vin = vsrc.rearrange("k p c -> p k c")
                                else:
                                    vsrc = kvV_loc.ap()[chv * CH:(chv + 1) * CH,
                                                        hg * 256:(hg + 1) * 256]
                                    vin = vsrc.rearrange("(kt p) c -> p kt c", kt=4)
                                q.dma_start(
                                    out=vst[M % 3][:, :].rearrange("p (kt c) -> p kt c", kt=4),
                                    in_=vin,
                                ).then_inc(sSv[M % 3].h, 16)
                                sSt[M % 3].n += 16
                                sSv[M % 3].n += 16

                    @blk.tensor
                    def _(te, chv=chv, nsl=nsl, ngr=ngr, gk0=gk0, gm0=gm0,
                          bVm=bVm, bV=bV):
                        if chv == 0:
                            te.wait_ge(sV.h, BIAS_V)
                        else:
                            te.wait_ge(sV.h, bV)
                        nvm = bVm

                        def emit_bcast(hgp):
                            te.wait_ge(sV.h, bV + hgp * 4 + 1)
                            if hgp >= 1:
                                te.wait_ge(sV.h, bV + (hgp - 1) * 4 + 2)
                            for hi in range(2):
                                te.matmul(psDR2[:, hi * CH:(hi + 1) * CH],
                                          ones65f[hi * 64:hi * 64 + 1, :],
                                          rb2[hi * 64:hi * 64 + 1, :], start=True,
                                          stop=True).then_inc(sT.h, 1)

                        for hg in range(NHG):
                            grp0 = gk0 + hg * ngr
                            for r in range(ngr + 1):
                                Gg = grp0 + r
                                m = r // 2
                                if r < ngr:
                                    M = gm0 + hg * nsl + m
                                    if Gg >= 1:
                                        te.wait_ge(sSe.h, Gg)
                                    if r % 2 == 0:
                                        te.wait_ge(sSt[M % 3].h, 16 * (M // 3 + 1))
                                    for ktl in range(2):
                                        kt = (r % 2) * 2 + ktl
                                        for hi in range(2):
                                            last = te.matmul(
                                                psS[:, (ktl * 2 + hi) * CH:
                                                    (ktl * 2 + hi + 1) * CH],
                                                kst[M % 3][:, hi * CH + kt * 128:
                                                           hi * CH + (kt + 1) * 128],
                                                ar2[:, (2 * hg + hi) * T + chv * CH:
                                                    (2 * hg + hi) * T + (chv + 1) * CH],
                                                start=True, stop=True)
                                    last.then_inc(sT.h, 1)
                                    if r == 0 and hg >= 1:
                                        emit_bcast(hg - 1)
                                if r >= 1:
                                    Gp = Gg - 1
                                    rp = r - 1
                                    mp = rp // 2
                                    Mp = gm0 + hg * nsl + mp
                                    if rp % 2 == 0:
                                        te.wait_ge(sSv[Mp % 3].h, 16 * (Mp // 3 + 1))
                                    diag_p = (mp == nsl - 1)
                                    if diag_p:
                                        nvm += 1
                                        te.wait_ge(sVm.h, nvm)
                                    elif r == ngr:
                                        te.wait_ge(sSe.h, Gp + 1)
                                    if rp == 0:
                                        te.wait_ge(sV.h, bV + hg * 4)
                                    last = None
                                    for ktl in range(2):
                                        ktp = (rp % 2) * 2 + ktl
                                        st = rp == 0 and ktl == 0
                                        sp = rp == ngr - 1 and ktl == 1
                                        for hi in range(2):
                                            av_i = te.matmul(
                                                psAVV[:, hi * CH:(hi + 1) * CH],
                                                vst[Mp % 3][:, ktp * 256 + hi * 128:
                                                            ktp * 256 + (hi + 1) * 128],
                                                ptb[Gp % 2][:, (ktl * 2 + hi) * CH:
                                                            (ktl * 2 + hi + 1) * CH],
                                                start=st, stop=sp)
                                            if ktl == 1 and hi == 1 and rp % 2 == 1:
                                                av_i.then_inc(sTc.h, 1)
                                            last = te.matmul(
                                                psDh[hi], ones128[:, :],
                                                ptb[Gp % 2][:, (ktl * 2 + hi) * CH:
                                                            (ktl * 2 + hi + 1) * CH],
                                                start=st, stop=sp)
                                    last.then_inc(sTa.h, 1)
                        emit_bcast(NHG - 1)
                        sT.n += NHG * (ngr + 2)
                        sTa.n += NHG * ngr
                        sTc.n += NHG * nsl
                        sVm.n = nvm

                    @blk.scalar
                    def _(sc, chv=chv, nsl=nsl, ngr=ngr, gk0=gk0, bTs=bTs):
                        if chv == 0:
                            sc.wait_ge(sX.h, TRI_X)
                        for hg in range(NHG):
                            for r in range(ngr):
                                Gg = gk0 + hg * ngr + r
                                m = r // 2
                                sc.wait_ge(sT.h, pos_sc[(hg, r)])
                                if Gg >= 2:
                                    sc.wait_ge(sTa.h, Gg - 1)
                                if m == nsl - 1:
                                    col = 512 + chv * 16 + hg
                                else:
                                    col = chv * 256 + hg * 16 + m
                                sc.activation(out=ptb[Gg % 2][:, :], in_=psS[:, :],
                                              func=AF.Exp, bias=biasb[:, col:col + 1],
                                              scale=SCALE).then_inc(sSe.h, 1)
                        sSe.n += NHG * ngr

                    @blk.vector
                    def _(v, chv=chv, nsl=nsl, ngr=ngr, gk0=gk0, bTa=bTa, bV=bV, bTs=bTs):
                        for hg in range(NHG):
                            for r in (ngr - 2, ngr - 1):
                                Gg = gk0 + hg * ngr + r
                                v.wait_ge(sSe.h, Gg + 1)
                                last = None
                                for ktl in range(2):
                                    kt = (r % 2) * 2 + ktl
                                    last = v.tensor_mul(
                                        ptb[Gg % 2][:, ktl * 1024:(ktl + 1) * 1024]
                                        .rearrange("p (hi c) -> p hi c", hi=2),
                                        ptb[Gg % 2][:, ktl * 1024:(ktl + 1) * 1024]
                                        .rearrange("p (hi c) -> p hi c", hi=2),
                                        tri_s[:, kt * CH:(kt + 1) * CH].unsqueeze(1)
                                        .broadcast_to([128, 2, CH]))
                                last.then_inc(sVm.h, 1)
                            v.wait_ge(sTa.h, bTa + (hg + 1) * ngr)
                            v.reciprocal(rb2[0:65, :],
                                         psDR2[0:65, 0:CH]).then_inc(sV.h, 1)
                            v.wait_ge(sT.h, pos_bc[(hg, 1)])
                            v.tensor_copy(rbbs2[:, :], psDR2[:, :]).then_inc(sV.h, 1)
                            v.wait_ge(sV.h, bV + hg * 4 + 2)
                            v.tensor_mul(attno[:, 2 * hg * CH:(2 * hg + 2) * CH],
                                         psAVV[:, :], rbbs2[:, :]).then_inc(sV.h, 2)
                        sV.n += NHG * 4

                if stop_phase <= 11 + 2 * chv:
                    return nc
                gki[0] += NHG * ngr
                gmi[0] += NHG * nsl
                P5CH_V = sV.n

                bV, bSc, bT0, bG = sV.n, sSc.n, sT.n, sG.n
                bW = [sW[i].n for i in range(4)]
                bX6 = sX.n
                bOut6 = sOut.n
                with nc.Block() as blk:
                    @blk.sync
                    def _(q, bSc=bSc):
                        for f in range(FC):
                            for hh in range(4):
                                m = 4 * f + hh
                                if m >= 4:
                                    mp = m - 4
                                    q.wait_ge(sSc.h, bSc + 4 * (mp // 4) + (mp % 4) + 1)
                                srcw = wo_t[:, f, hh * 1024:(hh + 1) * 1024]
                                q.dma_start(out=w32a[m % 4][0:64, :],
                                            in_=srcw[0:64, :]).then_inc(sW[m % 4].h, 16)
                                q.dma_start(out=w32a[m % 4][64:128, :],
                                            in_=srcw[64:128, :]).then_inc(sW[m % 4].h, 16)
                                sW[m % 4].n += 32

                    @blk.scalar
                    def _(sc, bW=bW, bT0=bT0):
                        for f in range(FC):
                            for hh in range(4):
                                m = 4 * f + hh
                                par = m % 4
                                sc.wait_ge(sW[par].h, bW[par] + 32 * (m // 4 + 1))
                                if f >= 3 and hh == 0:
                                    sc.wait_ge(sT.h, bT0 + (f - 2))
                                sc.copy(wba[f % 3][:, hh * 1024:(hh + 1) * 1024],
                                        w32a[par][:, :]).then_inc(sSc.h, 1)
                    sSc.n += FC * 4

                    @blk.gpsimd
                    def _(g, chv=chv, bV=bV, bOut=bOut6):
                        for f in range(FC):
                            if f >= 2:
                                g.wait_ge(sV.h, bV + 2 * (f - 2) + 2)
                            g.dma_start(out=xtb[f % 2][:, :],
                                        in_=xT_own[f * 128:(f + 1) * 128,
                                                   chv * CH:(chv + 1) * CH]
                                        ).then_inc(sX.h, 16)
                            g.dma_start(out=gtl[f % 2][:, :],
                                        in_=g_sp.ap()[f * 128:(f + 1) * 128,
                                                      chv * CH:(chv + 1) * CH]
                                        ).then_inc(sX.h, 16)
                            if f >= 1:
                                g.wait_ge(sV.h, bV + 2 * (f - 1) + 2)
                                g.dma_start(out=outT[(f - 1) * 128:f * 128,
                                                     chv * CH:(chv + 1) * CH],
                                            in_=otl[(f - 1) % 2][:, :]
                                            ).then_inc(sOut.h, 16)
                        g.wait_ge(sV.h, bV + 2 * FC)
                        g.dma_start(out=outT[(FC - 1) * 128:FC * 128,
                                             chv * CH:(chv + 1) * CH],
                                    in_=otl[(FC - 1) % 2][:, :]).then_inc(sOut.h, 16)
                    sX.n += FC * 32
                    sOut.n += FC * 16

                    @blk.tensor
                    def _(te, bSc=bSc, bV=bV):
                        te.wait_ge(sV.h, P5CH_V)
                        for f in range(FC):
                            te.wait_ge(sSc.h, bSc + 4 * f + 4)
                            if f >= 2:
                                te.wait_ge(sV.h, bV + 2 * (f - 2) + 1)
                            for hh in range(H):
                                last = te.matmul(psS[:, (f % 2) * CH:(f % 2 + 1) * CH],
                                                 wba[f % 3][:, hh * 128:(hh + 1) * 128],
                                                 attno[:, hh * CH:(hh + 1) * CH],
                                                 start=(hh == 0), stop=(hh == H - 1))
                            last.then_inc(sT.h, 1)
                    sT.n += FC

                    @blk.vector
                    def _(v, bT0=bT0, bX=bX6, bV=bV, bOut=bOut6):
                        for f in range(FC):
                            v.wait_ge(sT.h, bT0 + f + 1)
                            v.wait_ge(sX.h, bX + (f + 1) * 32)
                            v.tensor_mul(sgf[f % 2][:, :],
                                         psS[:, (f % 2) * CH:(f % 2 + 1) * CH],
                                         gtl[f % 2][:, :]).then_inc(sV.h, 1)
                            v.wait_ge(sV.h, bV + 2 * f + 1)
                            if f >= 2:
                                v.wait_ge(sOut.h, bOut + 16 * (f - 1))
                            v.tensor_add(otl[f % 2][:, :], sgf[f % 2][:, :],
                                         xtb[f % 2][:, :]).then_inc(sV.h, 1)
                    sV.n += FC * 2
                if stop_phase <= 12 + 2 * chv:
                    return nc

        with nc.Block() as blk:
            @blk.sync
            def _(q):
                q.wait_ge(sOut.h, sOut.n)

    return nc


def _prep_weights(wq, wk, wv, wo, Wb, Wr, A, B, bb):
    perm = np.concatenate([np.arange(0, 128, 2), np.arange(1, 128, 2)])
    full_perm = np.concatenate([h * 128 + perm for h in range(H)])

    def tile_fmaj(w):
        return np.ascontiguousarray(
            w.reshape(FC, 128, FC, 128).transpose(1, 2, 0, 3).reshape(128, FC, D))

    wq_p = wq[:, full_perm]
    wk_p = wk[:, full_perm]
    wq_t = tile_fmaj(wq_p)
    wk_t = tile_fmaj(wk_p)
    wv_t = tile_fmaj(wv)
    wo_t = tile_fmaj(wo)
    wb_t = tile_fmaj(np.ascontiguousarray(Wb.T))
    wr_t = np.ascontiguousarray(Wr.T.reshape(FC, 128, 8).transpose(1, 0, 2).reshape(128, FC * 8))
    A_f = A.reshape(128, D)
    a_t = np.ascontiguousarray(A_f.T.reshape(FC, 128, 128).transpose(1, 0, 2).reshape(128, FC * 128))
    b_t = np.ascontiguousarray(B.transpose(0, 2, 1).reshape(128, D))
    bb_t = np.ascontiguousarray(bb.reshape(FC, 128).T)
    return wq_t, wk_t, wv_t, wo_t, wb_t, wr_t, a_t, b_t, bb_t


def kernel(L0_attn_x, L0_attn_wq, L0_attn_wk, L0_attn_wv, L0_attn_wo,
           L0_ffn_W_base, L0_ffn_b_base, L0_ffn_W_router, L0_ffn_A_stack,
           L0_ffn_B_stack):
    x = np.asarray(L0_attn_x, dtype=np.float32).reshape(S := 8192, D)
    wq = np.asarray(L0_attn_wq, dtype=np.float32)
    wk = np.asarray(L0_attn_wk, dtype=np.float32)
    wv = np.asarray(L0_attn_wv, dtype=np.float32)
    wo = np.asarray(L0_attn_wo, dtype=np.float32)
    Wb = np.asarray(L0_ffn_W_base, dtype=np.float32)
    bb = np.asarray(L0_ffn_b_base, dtype=np.float32)
    Wr = np.asarray(L0_ffn_W_router, dtype=np.float32)
    A = np.asarray(L0_ffn_A_stack, dtype=np.float32)
    B = np.asarray(L0_ffn_B_stack, dtype=np.float32)

    wq_t, wk_t, wv_t, wo_t, wb_t, wr_t, a_t, b_t, bb_t = _prep_weights(
        wq, wk, wv, wo, Wb, Wr, A, B, bb)

    inv = 1.0 / (10000.0 ** (np.arange(0, 128, 2)[:64].astype(np.float32) / 128.0))

    nc = build_program()

    in_maps = []
    for c in range(NC):
        rows = np.concatenate([np.arange(512 * c, 512 * (c + 1)),
                               np.arange(512 * (15 - c), 512 * (16 - c))])
        x_own = np.ascontiguousarray(x[rows])
        xT_own = np.ascontiguousarray(x_own.T)
        ang = np.outer(rows.astype(np.float32), inv).astype(np.float32)
        cs = np.cos(ang).T
        sn = np.sin(ang).T
        cos_in = np.ascontiguousarray(np.concatenate([cs, cs], axis=0))
        sin_in = np.ascontiguousarray(np.concatenate([-sn, sn], axis=0))
        kill = np.full((1, 32), 0.0, dtype=np.float32)
        for j in range(16):
            kill[0, j] = 0.0 if j < c else NEG
            kill[0, 16 + j] = 0.0 if j < 15 - c else NEG
        in_maps.append({
            "x_own": x_own, "xT_own": xT_own,
            "wq_t": wq_t, "wk_t": wk_t, "wv_t": wv_t, "wo_t": wo_t,
            "wb_t": wb_t, "wr_t": wr_t, "a_t": a_t, "b_t": b_t, "bb_t": bb_t,
            "cos_in": cos_in, "sin_in": sin_in, "kill_in": kill,
        })

    res = run_bass_kernel_spmd(nc, in_maps, core_ids=list(range(NC)))

    out = np.empty((S, D), dtype=np.float32)
    for c in range(NC):
        oT = np.asarray(res.results[c]["outT"])
        out[512 * c:512 * (c + 1)] = oT[:, 0:CH].T
        out[512 * (15 - c):512 * (16 - c)] = oT[:, CH:T].T
    return out.reshape(1, S, D)


if __name__ == "__main__":
    pass
